# version 17
# speedup vs baseline: 1.3498x; 1.0310x over previous
import numpy as np
import ml_dtypes
from contextlib import ExitStack

import concourse.bass as bass
import concourse.mybir as mybir
from concourse.bass_utils import run_bass_kernel_spmd

F32 = mybir.dt.float32
BF16 = mybir.dt.bfloat16
AF = mybir.ActivationFunctionType
ALU = mybir.AluOpType
AX = mybir.AxisListType.X

MAGIC = 12582912.0
EPS = 1e-5
NEGBIG = -1e30
D = 2048
HID = 5632
HCH = HID // 512
SEQ = 2048
NT = 1024
SCL = float(1.0 / np.sqrt(np.float32(128.0), dtype=np.float32))

ASET = [0, 1, 4, 5, 8, 9, 12, 13]
BSET = [2, 3, 6, 7, 10, 11, 14, 15]
EXT = [512, 512, 1024, 1024, 1536, 1536, 2048, 2048]

ENG_SEM = {"sync": "qS", "gpsimd": "qG", "tensor": "sT", "scalar": "sSc", "vector": "sV",
           "cc": "cc", "qA": "qA"}
COMPUTE = ("tensor", "scalar", "vector")

SET_OF = {}
LT_OF = {}
for _j, _g in enumerate(ASET):
    SET_OF[_g] = 0
    LT_OF[_g] = _j
for _j, _g in enumerate(BSET):
    SET_OF[_g] = 1
    LT_OF[_g] = _j


class Ctx:

    def __init__(self, nc, block, stack):
        self.nc = nc
        self.block = block
        self.cnt = {}
        self.sems = {}
        for k in set(ENG_SEM.values()):
            self.sems[k] = stack.enter_context(nc.semaphore(k))
            self.cnt[k] = 0
        self.res_w = {}
        self.res_r = {}
        self.known = {e: {} for e in ENG_SEM}
        self.arena = {}
        self.buf = []

    def on(self, eng, f):
        self.buf.append((eng, f))

    def raw(self, eng, f):
        self.buf.append((eng, f))

    def flush(self):
        per = {}
        for eng, f in self.buf:
            per.setdefault(eng, []).append(f)
        self.buf = []
        CH = 2000
        for eng, fns in per.items():
            for i in range(0, len(fns), CH):
                chunk = fns[i:i + CH]
                getattr(self.block, eng)(lambda e, chunk=chunk: [f(e) for f in chunk])

    def _wait(self, eng, key, val):
        if val <= 0:
            return
        if eng == "tensor" and key == ENG_SEM[eng]:
            return
        kn = self.known[eng]
        if kn.get(key, 0) >= val:
            return
        kn[key] = val
        h = self.sems[key]
        self.on(eng, lambda e: e.wait_ge(h, val))

    def _acquire(self, eng, reads, writes):
        for r in reads:
            w = self.res_w.get(r)
            if w:
                self._wait(eng, w[0], w[1])
        for r in writes:
            w = self.res_w.get(r)
            if w:
                self._wait(eng, w[0], w[1])
            for k, v in self.res_r.get(r, {}).items():
                self._wait(eng, k, v)
        if writes:
            for k, v in self.arena.items():
                self._wait(eng, k, v)

    def _record(self, key, val, reads, writes):
        for r in writes:
            self.res_w[r] = (key, val)
            self.res_r[r] = {}
        for r in reads:
            d = self.res_r.setdefault(r, {})
            if d.get(key, 0) < val:
                d[key] = val

    def op(self, eng, f, reads=(), writes=()):
        key = ENG_SEM[eng]
        self._acquire(eng, reads, writes)
        self.cnt[key] += 1
        val = self.cnt[key]
        h = self.sems[key]
        self.on(eng, lambda e: f(e).then_inc(h, 1))
        if eng == "tensor":
            self.known[eng][key] = val
        self._record(key, val, reads, writes)
        return val

    def dma(self, qeng, out, in_, reads=(), writes=(), **kw):
        key = "qA" if qeng == "scalar" else ENG_SEM[qeng]
        self._acquire(qeng, reads, writes)
        self.cnt[key] += 16
        val = self.cnt[key]
        h = self.sems[key]
        self.on(qeng, lambda q: q.dma_start(out=out, in_=in_, **kw).then_inc(h, 16))
        self._record(key, val, reads, writes)
        return val

    def collective(self, kind, op, groups, ins, outs, reads=(), writes=()):
        self._acquire("gpsimd", reads, writes)
        self.cnt["cc"] += 1
        val = self.cnt["cc"]
        h = self.sems["cc"]
        self.on("gpsimd", lambda g: g.collective_compute(
            kind, op, replica_groups=groups, ins=ins, outs=outs).then_inc(h, 1))
        self._record("cc", val, reads, writes)
        return val

    def free_barrier(self, resources):
        for r in resources:
            w = self.res_w.pop(r, None)
            if w and self.arena.get(w[0], 0) < w[1]:
                self.arena[w[0]] = w[1]
            for k, v in self.res_r.pop(r, {}).items():
                if self.arena.get(k, 0) < v:
                    self.arena[k] = v

    def finish(self, eng):
        for k, v in self.cnt.items():
            h = self.sems[k]
            if v > 0:
                self.on(eng, lambda e, h=h, v=v: e.wait_ge(h, v))


def sb(stack, nc, name, shape, dt, **kw):
    return stack.enter_context(nc.sbuf_tensor(name, shape, dt, **kw))


def emit_ln_stats(ctx, st, c0, x_res, xt_ap, sqdump, sqdump_res, width=D):
    stres = f"st{c0}"

    def col(i):
        return st[:, c0 + i:c0 + i + 1]

    ctx.op("vector", lambda v: v.reduce_sum(out=col(0), in_=xt_ap, axis=AX),
           reads=(x_res,), writes=(stres,))
    ctx.op("vector", lambda v: v.reduce_max(out=col(1), in_=xt_ap, axis=AX),
           reads=(x_res,), writes=(stres,))
    ctx.op("vector", lambda v: v.tensor_reduce(out=col(2), in_=xt_ap, axis=AX, op=ALU.min),
           reads=(x_res,), writes=(stres,))
    ctx.op("scalar", lambda s: s.activation(out=sqdump, in_=xt_ap, func=AF.Square,
                                            accum_out=col(3)),
           reads=(x_res,), writes=(sqdump_res, stres))
    ctx.op("vector", lambda v: v.tensor_scalar(out=col(4), in0=col(0), scalar1=1.0 / width,
                                               scalar2=None, op0=ALU.mult),
           reads=(stres,), writes=(stres,))
    ctx.op("vector", lambda v: v.tensor_scalar(out=col(5), in0=col(3), scalar1=1.0 / width,
                                               scalar2=None, op0=ALU.mult),
           reads=(stres,), writes=(stres,))
    ctx.op("vector", lambda v: v.tensor_tensor(out=col(6), in0=col(4), in1=col(4), op=ALU.mult),
           reads=(stres,), writes=(stres,))
    ctx.op("vector", lambda v: v.tensor_tensor(out=col(7), in0=col(5), in1=col(6), op=ALU.subtract),
           reads=(stres,), writes=(stres,))
    ctx.op("scalar", lambda s: s.activation(out=col(8), in_=col(7), func=AF.Sqrt,
                                            bias=ctx.epsc[:, 0:1], scale=1.0),
           reads=(stres, "epsc"), writes=(stres,))
    ctx.op("vector", lambda v: v.reciprocal(out=col(9), in_=col(8)),
           reads=(stres,), writes=(stres,))
    ctx.op("vector", lambda v: v.tensor_tensor(out=col(10), in0=col(1), in1=col(4), op=ALU.subtract),
           reads=(stres,), writes=(stres,))
    ctx.op("vector", lambda v: v.tensor_tensor(out=col(11), in0=col(4), in1=col(2), op=ALU.subtract),
           reads=(stres,), writes=(stres,))
    ctx.op("vector", lambda v: v.tensor_tensor(out=col(12), in0=col(10), in1=col(11), op=ALU.max),
           reads=(stres,), writes=(stres,))
    ctx.op("vector", lambda v: v.tensor_tensor(out=col(13), in0=col(12), in1=col(9), op=ALU.mult),
           reads=(stres,), writes=(stres,))
    ctx.op("vector", lambda v: v.tensor_scalar(out=col(13), in0=col(13), scalar1=EPS,
                                               scalar2=None, op0=ALU.max),
           reads=(stres,), writes=(stres,))
    ctx.op("vector", lambda v: v.reciprocal(out=col(14), in_=col(13)),
           reads=(stres,), writes=(stres,))
    ctx.op("vector", lambda v: v.tensor_scalar(out=col(14), in0=col(14), scalar1=127.0,
                                               scalar2=None, op0=ALU.mult),
           reads=(stres,), writes=(stres,))
    ctx.op("vector", lambda v: v.tensor_tensor(out=col(15), in0=col(9), in1=col(14), op=ALU.mult),
           reads=(stres,), writes=(stres,))
    ctx.op("vector", lambda v: v.scalar_tensor_tensor(out=col(16), in0=col(4), scalar=-1.0,
                                                      in1=col(15), op0=ALU.mult, op1=ALU.mult),
           reads=(stres,), writes=(stres,))
    ctx.op("vector", lambda v: v.scalar_tensor_tensor(out=col(19), in0=col(4), scalar=-1.0,
                                                      in1=col(9), op0=ALU.mult, op1=ALU.mult),
           reads=(stres,), writes=(stres,))


def build_ffn(quantize, debug=False):
    nc = bass.Bass("TRN2", target_bir_lowering=False)
    x = nc.dram_tensor("x", [NT, D], F32, kind="ExternalInput").ap()
    xo = nc.dram_tensor("xo", [NT, D], F32, kind="ExternalOutput").ap()
    if debug:
        dbg_xq = nc.dram_tensor("dbg_xq", [128, D], BF16, kind="ExternalOutput").ap()
        dbg_h = nc.dram_tensor("dbg_h", [128, HID], BF16, kind="ExternalOutput").ap()
        dbg_hq = nc.dram_tensor("dbg_hq", [128, HID], BF16, kind="ExternalOutput").ap()
        dbg_st = nc.dram_tensor("dbg_st", [128, 24], F32, kind="ExternalOutput").ap()
    if quantize:
        w1 = nc.dram_tensor("w1", [HID, D], F32, kind="ExternalInput").ap()
        w3 = nc.dram_tensor("w3", [HID, D], F32, kind="ExternalInput").ap()
        w2 = nc.dram_tensor("w2", [D, HID], F32, kind="ExternalInput").ap()
        w1q = nc.dram_tensor("w1q", [HID, D], BF16, kind="ExternalOutput").ap()
        w3q = nc.dram_tensor("w3q", [HID, D], BF16, kind="ExternalOutput").ap()
        w2q = nc.dram_tensor("w2q", [D, HID], BF16, kind="ExternalOutput").ap()
        scales_o = nc.dram_tensor("scales_o", [1, 8], F32, kind="ExternalOutput").ap()
    else:
        w1q = nc.dram_tensor("w1q", [HID, D], BF16, kind="ExternalInput").ap()
        w3q = nc.dram_tensor("w3q", [HID, D], BF16, kind="ExternalInput").ap()
        w2q = nc.dram_tensor("w2q", [D, HID], BF16, kind="ExternalInput").ap()
        scales_i = nc.dram_tensor("scales_i", [1, 8], F32, kind="ExternalInput").ap()
    scb = nc.dram_tensor("scb", [1, 8], F32).ap()

    with ExitStack() as stack:
        block = stack.enter_context(nc.Block())
        ctx = Ctx(nc, block, stack)

        xt = [sb(stack, nc, f"xt{t}", [128, D], F32) for t in range(8)]
        st = sb(stack, nc, "st", [128, 8 * 24], F32)
        sc_bc = sb(stack, nc, "sc_bc", [128, 8], F32)
        xq_tmp = sb(stack, nc, "xq_tmp", [128, D], BF16)
        ones = sb(stack, nc, "ones", [128, 1], F32)
        ctx.op("vector", lambda v: v.memset(ones[:], 1.0), writes=("ones",))
        ctx.epsc = sb(stack, nc, "epsc", [128, 1], F32)
        ctx.op("vector", lambda v: v.memset(ctx.epsc[:], EPS), writes=("epsc",))
        ps = [stack.enter_context(nc.psum_tensor(f"ps{i}", [128, 512], F32)) for i in range(8)]

        for t in range(8):
            ctx.dma("gpsimd", xt[t][:], x[128 * t:128 * (t + 1), :], writes=(f"xt{t}",))

        if quantize:
            with ExitStack() as wst:
                wld = [sb(wst, nc, f"wld{i}", [128, HID], F32) for i in range(2)]
                wqt = [sb(wst, nc, f"wqt{i}", [128, HID], BF16) for i in range(2)]
                p1b = sb(wst, nc, "p1b", [128, HID], F32)
                acc = sb(wst, nc, "acc", [128, 48], F32)
                sc_sm = sb(wst, nc, "sc_sm", [1, 8], F32)

                wsrc = [(w1, HID, D, w1q), (w3, HID, D, w3q), (w2, D, HID, w2q)]
                for wi, (w_ap, R, C, _) in enumerate(wsrc):
                    ntile = R // 128
                    for i in range(ntile):
                        par = i % 2
                        ctx.dma("sync", wld[par][:, 0:C], w_ap[128 * i:128 * (i + 1), :],
                                writes=(f"wld{par}",))
                        ctx.op("vector", lambda v, par=par, i=i, C=C: v.reduce_sum(
                            out=acc[:, i:i + 1], in_=wld[par][:, 0:C], axis=AX,
                            apply_absolute_value=True),
                            reads=(f"wld{par}",), writes=("acc",))
                    ctx.op("vector", lambda v, n=ntile: v.reduce_sum(
                        out=acc[:, 47:48], in_=acc[:, 0:n], axis=AX),
                        reads=("acc",), writes=("acc",))
                    ctx.op("tensor", lambda t, wi=wi: t.matmul(
                        ps[7][0:1, wi:wi + 1], ones[:], acc[:, 47:48], start=True, stop=True),
                        reads=("ones", "acc"), writes=("ps7",))
                    ctx.op("scalar", lambda s, wi=wi: s.copy(sc_sm[0:1, wi:wi + 1],
                                                             ps[7][0:1, wi:wi + 1]),
                           reads=("ps7",), writes=("sc_sm",))
                    ctx.op("scalar", lambda s, wi=wi, numel=float(R * C): s.mul(
                        sc_sm[0:1, wi:wi + 1], sc_sm[0:1, wi:wi + 1], 1.0 / numel),
                        reads=("sc_sm",), writes=("sc_sm",))
                ctx.op("vector", lambda v: v.tensor_scalar(out=sc_sm[0:1, 0:3], in0=sc_sm[0:1, 0:3],
                                                           scalar1=EPS, scalar2=None, op0=ALU.max),
                       reads=("sc_sm",), writes=("sc_sm",))
                ctx.op("vector", lambda v: v.reciprocal(out=sc_sm[0:1, 3:6], in_=sc_sm[0:1, 0:3]),
                       reads=("sc_sm",), writes=("sc_sm",))
                ctx.dma("gpsimd", scb[:], sc_sm[0:1, :], reads=("sc_sm",), writes=("scb",))
                ctx.dma("gpsimd", scales_o[:], sc_sm[0:1, :], reads=("sc_sm",))
                ctx.dma("gpsimd", sc_bc[:, 0:3], scb[0:1, 3:6].broadcast_to((128, 3)),
                        reads=("scb",), writes=("sc_bc",))
                ctx.dma("gpsimd", sc_bc[:, 3:6], scb[0:1, 0:3].broadcast_to((128, 3)),
                        reads=("scb",), writes=("sc_bc",))
                ctx.op("vector", lambda v: v.tensor_scalar(out=sc_bc[:, 3:5], in0=sc_bc[:, 3:5],
                                                           scalar1=1.0 / 127.0, scalar2=None,
                                                           op0=ALU.mult),
                       reads=("sc_bc",), writes=("sc_bc",))
                ctx.op("vector", lambda v: v.tensor_scalar(out=sc_bc[:, 5:6], in0=sc_bc[:, 5:6],
                                                           scalar1=0.5 / 127.0, scalar2=None,
                                                           op0=ALU.mult),
                       reads=("sc_bc",), writes=("sc_bc",))

                for wi, (w_ap, R, C, wq_ap) in enumerate(wsrc):
                    ntile = R // 128
                    for i in range(ntile):
                        par = i % 2
                        ctx.dma("sync", wld[par][:, 0:C], w_ap[128 * i:128 * (i + 1), :],
                                writes=(f"wld{par}",))
                        ctx.op("vector", lambda v, par=par, C=C, wi=wi: v.tensor_scalar(
                            out=p1b[:, 0:C], in0=wld[par][:, 0:C], scalar1=sc_bc[:, wi:wi + 1],
                            scalar2=MAGIC, op0=ALU.mult, op1=ALU.add),
                            reads=(f"wld{par}", "sc_bc"), writes=("p1b",))
                        ctx.op("vector", lambda v, C=C: v.tensor_scalar(
                            out=p1b[:, 0:C], in0=p1b[:, 0:C], scalar1=MAGIC, scalar2=1.0,
                            op0=ALU.subtract, op1=ALU.min),
                            reads=("p1b",), writes=("p1b",))
                        ctx.op("vector", lambda v, par=par, C=C: v.tensor_scalar(
                            out=wqt[par][:, 0:C], in0=p1b[:, 0:C], scalar1=-1.0, scalar2=None,
                            op0=ALU.max),
                            reads=("p1b",), writes=(f"wqt{par}",))
                        ctx.dma("gpsimd", wq_ap[128 * i:128 * (i + 1), :], wqt[par][:, 0:C],
                                reads=(f"wqt{par}",), writes=(f"wqdram{wi}",))
            ctx.free_barrier(["wld0", "wld1", "wqt0", "wqt1", "p1b", "acc", "sc_sm"])
        else:
            ctx.dma("gpsimd", sc_bc[:, 3:6], scales_i[0:1, 0:3].broadcast_to((128, 3)),
                    writes=("sc_bc",))
            ctx.op("vector", lambda v: v.tensor_scalar(out=sc_bc[:, 3:5], in0=sc_bc[:, 3:5],
                                                       scalar1=1.0 / 127.0, scalar2=None,
                                                       op0=ALU.mult),
                   reads=("sc_bc",), writes=("sc_bc",))
            ctx.op("vector", lambda v: v.tensor_scalar(out=sc_bc[:, 5:6], in0=sc_bc[:, 5:6],
                                                       scalar1=0.5 / 127.0, scalar2=None,
                                                       op0=ALU.mult),
                   reads=("sc_bc",), writes=("sc_bc",))

        with ExitStack() as fst:
            xqT = sb(fst, nc, "xqT", [128, 16 * 512], BF16, is_transpose_dest=True)
            hb = [sb(fst, nc, f"hb{k}", [128, HID], BF16) for k in range(4)]
            tmp512 = sb(fst, nc, "tmp512", [128, 512], F32)
            s1b = [sb(fst, nc, f"s1b{i}", [128, 512], F32) for i in range(2)]
            s3b = [sb(fst, nc, f"s3b{i}", [128, 512], F32) for i in range(2)]
            hmx = sb(fst, nc, "hmx", [128, 64], F32)

            for g in range(2):
                for k in range(4):
                    t = 4 * g + k
                    c0 = 24 * k
                    stres = f"st{c0}"
                    emit_ln_stats(ctx, st, c0, f"xt{t}", xt[t][:], xq_tmp[:], "xq_tmp")
                    ctx.op("vector", lambda v, c0=c0: v.tensor_tensor(
                        out=st[:, c0 + 17:c0 + 18], in0=st[:, c0 + 13:c0 + 14],
                        in1=sc_bc[:, 3:4], op=ALU.mult),
                        reads=(stres, "sc_bc"), writes=(stres,))
                    ctx.op("vector", lambda v, c0=c0: v.tensor_tensor(
                        out=st[:, c0 + 18:c0 + 19], in0=st[:, c0 + 13:c0 + 14],
                        in1=sc_bc[:, 4:5], op=ALU.mult),
                        reads=(stres, "sc_bc"), writes=(stres,))
                    for q4 in range(4):
                        sl = slice(512 * q4, 512 * (q4 + 1))
                        ctx.op("vector", lambda v, t=t, c0=c0, sl=sl: v.tensor_scalar(
                            out=tmp512[:], in0=xt[t][:, sl], scalar1=st[:, c0 + 15:c0 + 16],
                            scalar2=st[:, c0 + 16:c0 + 17], op0=ALU.mult, op1=ALU.add),
                            reads=(f"xt{t}", stres), writes=("tmp512",))
                        ctx.op("vector", lambda v, sl=sl: v.tensor_scalar(
                            out=xq_tmp[:, sl], in0=tmp512[:], scalar1=MAGIC, scalar2=MAGIC,
                            op0=ALU.add, op1=ALU.subtract),
                            reads=("tmp512",), writes=("xq_tmp",))
                    if debug and g == 0 and k == 0:
                        ctx.dma("gpsimd", dbg_xq[:, :], xq_tmp[:, :], reads=("xq_tmp",))
                    for c in range(16):
                        ctx.dma("sync", xqT[:, 512 * c + 128 * k:512 * c + 128 * (k + 1)],
                                xq_tmp[:, 128 * c:128 * (c + 1)], transpose=True,
                                reads=("xq_tmp",), writes=("xqT",))

                with ExitStack() as wsc:
                    w1T = [sb(wsc, nc, f"w1T{i}_g{g}", [128, 16 * 512], BF16,
                              is_transpose_dest=True) for i in range(2)]
                    w3T = [sb(wsc, nc, f"w3T{i}_g{g}", [128, 16 * 512], BF16,
                              is_transpose_dest=True) for i in range(2)]

                    def load_w13(hc):
                        par = hc % 2
                        for c in range(16):
                            ctx.dma("sync", w1T[par][:, 512 * c:512 * (c + 1)],
                                    w1q[512 * hc:512 * (hc + 1), 128 * c:128 * (c + 1)],
                                    transpose=True, reads=("wqdram0",), writes=(f"w1T{par}",))
                            ctx.dma("sync", w3T[par][:, 512 * c:512 * (c + 1)],
                                    w3q[512 * hc:512 * (hc + 1), 128 * c:128 * (c + 1)],
                                    transpose=True, reads=("wqdram1",), writes=(f"w3T{par}",))

                    load_w13(0)
                    load_w13(1)
                    for hc in range(HCH):
                        par = hc % 2
                        for k in range(4):
                            b1, b3 = 2 * (k % 2), 2 * (k % 2) + 1

                            def mms(te, k=k, par=par, b1=b1, b3=b3):
                                ins = None
                                for c in range(16):
                                    lhs = xqT[:, 512 * c + 128 * k:512 * c + 128 * (k + 1)]
                                    te.matmul(ps[b1][:, :], lhs, w1T[par][:, 512 * c:512 * (c + 1)],
                                              start=(c == 0), stop=(c == 15))
                                    ins = te.matmul(ps[b3][:, :], lhs,
                                                    w3T[par][:, 512 * c:512 * (c + 1)],
                                                    start=(c == 0), stop=(c == 15))
                                return ins
                            ctx.op("tensor", mms,
                                   reads=("xqT", f"w1T{par}", f"w3T{par}"),
                                   writes=(f"ps{b1}", f"ps{b3}"))
                            c0 = 24 * k
                            sp = (hc * 4 + k) % 2
                            ctx.op("scalar", lambda s, b1=b1, c0=c0, sp=sp: s.activation(
                                out=s1b[sp][:], in_=ps[b1][:, :], func=AF.Silu,
                                scale=st[:, c0 + 17:c0 + 18]),
                                reads=(f"ps{b1}", f"st{c0}"), writes=(f"s1b{sp}",))
                            ctx.op("scalar", lambda s, b3=b3, c0=c0, sp=sp: s.activation(
                                out=s3b[sp][:], in_=ps[b3][:, :], func=AF.Identity,
                                scale=st[:, c0 + 18:c0 + 19]),
                                reads=(f"ps{b3}", f"st{c0}"), writes=(f"s3b{sp}",))
                            ctx.op("vector", lambda v, k=k, hc=hc, sp=sp: v.tensor_tensor(
                                out=hb[k][:, 512 * hc:512 * (hc + 1)], in0=s1b[sp][:],
                                in1=s3b[sp][:], op=ALU.mult),
                                reads=(f"s1b{sp}", f"s3b{sp}"), writes=(f"hb{k}",))
                            ctx.op("vector", lambda v, k=k, hc=hc: v.reduce_max(
                                out=hmx[:, 16 * k + hc:16 * k + hc + 1],
                                in_=hb[k][:, 512 * hc:512 * (hc + 1)], axis=AX,
                                apply_absolute_value=True),
                                reads=(f"hb{k}",), writes=("hmx",))
                        if hc + 2 < HCH:
                            load_w13(hc + 2)
                    ctx.free_barrier(["w1T0", "w1T1", "w3T0", "w3T1"])

                with ExitStack() as w2s:
                    hqT = sb(w2s, nc, f"hqT_g{g}", [128, 44 * 512], BF16, is_transpose_dest=True)
                    w2b = [sb(w2s, nc, f"w2b{i}_g{g}", [128, 1024], BF16, is_transpose_dest=True)
                           for i in range(2)]
                    obuf = [sb(w2s, nc, f"obuf{i}_g{g}", [128, 512], F32) for i in range(2)]

                    if debug and g == 0:
                        ctx.dma("gpsimd", dbg_h[:, :], hb[0][:, :], reads=("hb0",))
                    for k in range(4):
                        c0 = 24 * k
                        stres = f"st{c0}"
                        ctx.op("vector", lambda v, k=k, c0=c0: v.reduce_max(
                            out=st[:, c0 + 20:c0 + 21], in_=hmx[:, 16 * k:16 * k + HCH], axis=AX),
                            reads=("hmx",), writes=(stres,))
                        ctx.op("vector", lambda v, c0=c0: v.tensor_scalar(
                            out=st[:, c0 + 20:c0 + 21], in0=st[:, c0 + 20:c0 + 21],
                            scalar1=EPS, scalar2=None, op0=ALU.max),
                            reads=(stres,), writes=(stres,))
                        ctx.op("vector", lambda v, c0=c0: v.reciprocal(
                            out=st[:, c0 + 21:c0 + 22], in_=st[:, c0 + 20:c0 + 21]),
                            reads=(stres,), writes=(stres,))
                        ctx.op("vector", lambda v, c0=c0: v.tensor_scalar(
                            out=st[:, c0 + 21:c0 + 22], in0=st[:, c0 + 21:c0 + 22],
                            scalar1=127.0, scalar2=None, op0=ALU.mult),
                            reads=(stres,), writes=(stres,))
                        ctx.op("vector", lambda v, c0=c0: v.tensor_tensor(
                            out=st[:, c0 + 22:c0 + 23], in0=st[:, c0 + 20:c0 + 21],
                            in1=sc_bc[:, 5:6], op=ALU.mult),
                            reads=(stres, "sc_bc"), writes=(stres,))
                        for hc in range(HCH):
                            ctx.op("vector", lambda v, k=k, hc=hc, c0=c0: v.tensor_scalar(
                                out=tmp512[:], in0=hb[k][:, 512 * hc:512 * (hc + 1)],
                                scalar1=st[:, c0 + 21:c0 + 22], scalar2=MAGIC,
                                op0=ALU.mult, op1=ALU.add),
                                reads=(f"hb{k}", stres), writes=("tmp512",))
                            ctx.op("vector", lambda v, k=k, hc=hc: v.tensor_scalar(
                                out=hb[k][:, 512 * hc:512 * (hc + 1)], in0=tmp512[:],
                                scalar1=MAGIC, scalar2=None, op0=ALU.subtract),
                                reads=("tmp512",), writes=(f"hb{k}",))
                        if debug and g == 0 and k == 0:
                            ctx.dma("gpsimd", dbg_hq[:, :], hb[0][:, :], reads=("hb0",))
                            ctx.dma("gpsimd", dbg_st[:, :], st[:, 0:24], reads=("st0",))
                        for c in range(44):
                            ctx.dma("sync", hqT[:, 512 * c + 128 * k:512 * c + 128 * (k + 1)],
                                    hb[k][:, 128 * c:128 * (c + 1)], transpose=True,
                                    reads=(f"hb{k}",), writes=("hqT",))

                    for dcp in range(2):
                        def load_w2(hch, dcp=dcp):
                            ctx.dma("sync", w2b[hch % 2][:, :],
                                    w2q[1024 * dcp:1024 * (dcp + 1), 128 * hch:128 * (hch + 1)],
                                    transpose=True, reads=("wqdram2",),
                                    writes=(f"w2b{hch % 2}",))
                        load_w2(0)
                        load_w2(1)
                        for hch in range(44):
                            def mm2(te, hch=hch):
                                ins = None
                                for k in range(4):
                                    lhs = hqT[:, 512 * hch + 128 * k:512 * hch + 128 * (k + 1)]
                                    for dci in range(2):
                                        ins = te.matmul(ps[k * 2 + dci][:, :], lhs,
                                                        w2b[hch % 2][:, 512 * dci:512 * (dci + 1)],
                                                        start=(hch == 0), stop=(hch == 43))
                                return ins
                            ctx.op("tensor", mm2,
                                   reads=("hqT", f"w2b{hch % 2}"),
                                   writes=tuple(f"ps{i}" for i in range(8)))
                            if hch + 2 < 44:
                                load_w2(hch + 2)
                        for k in range(4):
                            t = 4 * g + k
                            c0 = 24 * k
                            for dci in range(2):
                                j = (k * 2 + dci) % 2
                                ctx.op("scalar", lambda s, k=k, dci=dci, j=j, c0=c0: s.activation(
                                    out=obuf[j][:], in_=ps[k * 2 + dci][:, :], func=AF.Identity,
                                    scale=st[:, c0 + 22:c0 + 23]),
                                    reads=(f"ps{k * 2 + dci}", f"st{c0}"), writes=(f"obuf{j}",))
                                lo = 1024 * dcp + 512 * dci
                                ctx.op("vector", lambda v, t=t, lo=lo, j=j: v.tensor_tensor(
                                    out=xt[t][:, lo:lo + 512], in0=xt[t][:, lo:lo + 512],
                                    in1=obuf[j][:], op=ALU.add),
                                    reads=(f"obuf{j}",), writes=(f"xt{t}",))
                    ctx.free_barrier(["hqT", "w2b0", "w2b1", "obuf0", "obuf1"])

                for k in range(4):
                    t = 4 * g + k
                    ctx.dma("gpsimd", xo[128 * t:128 * (t + 1), :], xt[t][:], reads=(f"xt{t}",))

        ctx.finish("gpsimd")
        ctx.flush()
    return nc


def build_attn():
    nc = bass.Bass("TRN2", target_bir_lowering=False)
    xm = nc.dram_tensor("xm", [SEQ, D], F32, kind="ExternalInput").ap()
    xmo = nc.dram_tensor("xmo", [NT, D], F32, kind="ExternalInput").ap()
    masks = nc.dram_tensor("masks", [8, 128, 512], F32, kind="ExternalInput").ap()
    shp = {"w_kv": [D, 512], "b_kv": [512], "w_ku": [512, 1024], "b_ku": [1024],
           "w_kr": [D, 1024], "b_kr": [1024], "w_vu": [512, D], "b_vu": [D],
           "w_q": [D, 512], "b_q": [512], "w_qu": [512, 1024], "b_qu": [1024],
           "w_qr": [512, 1024], "b_qr": [1024], "w_o": [D, D], "b_o": [D]}
    dr = {n: nc.dram_tensor(n, s, F32, kind="ExternalInput").ap() for n, s in shp.items()}
    xout = nc.dram_tensor("xout", [NT, D], F32, kind="ExternalOutput").ap()
    vscr = nc.dram_tensor("vscr", [D, SEQ], BF16).ap()
    qscr = nc.dram_tensor("qscr", [D, NT], BF16).ap()
    ascr = nc.dram_tensor("ascr", [D, NT], BF16).ap()

    with ExitStack() as stack:
        block = stack.enter_context(nc.Block())
        ctx = Ctx(nc, block, stack)

        st = sb(stack, nc, "st", [128, 24], F32)
        ctx.epsc = sb(stack, nc, "epsc", [128, 1], F32)
        ctx.op("vector", lambda v: v.memset(ctx.epsc[:], EPS), writes=("epsc",))
        bias = {}
        for n, s in shp.items():
            if n.startswith("b_") and n != "b_o":
                nch = s[0] // 128
                bias[n] = sb(stack, nc, f"sb{n}", [128, nch], F32)
                bv = dr[n].rearrange("(c p) -> c p", p=128)
                for c in range(nch):
                    ctx.dma("gpsimd", bias[n][:, c:c + 1], bv[c:c + 1, :].rearrange("a p -> p a"),
                            writes=(n,))
        ckv = [sb(stack, nc, f"ckv{c}", [128, SEQ], BF16) for c in range(4)]
        qlat = [sb(stack, nc, f"qlat{c}", [128, NT], BF16) for c in range(4)]
        ps = [stack.enter_context(nc.psum_tensor(f"ps{i}", [128, 512], F32)) for i in range(8)]

        def proj(wtiles, wres, nkc, rhs, rhs_res, rhs_len, outf, bname, ores):
            nch = len(wtiles)
            for kc in range(nkc):
                for tch in range(rhs_len // 512):
                    bank = (kc * (rhs_len // 512) + tch) % 2

                    def mmp(te, kc=kc, tch=tch, bank=bank):
                        ins = None
                        for ch in range(nch):
                            ins = te.matmul(ps[bank][:, :],
                                            wtiles[ch][:, 128 * kc:128 * (kc + 1)],
                                            rhs[ch][:, 512 * tch:512 * (tch + 1)],
                                            start=(ch == 0), stop=(ch == nch - 1))
                        return ins
                    ctx.op("tensor", mmp,
                           reads=tuple(f"{wres}{ch}" for ch in range(nch)) +
                                 tuple(f"{rhs_res}{ch}" for ch in range(nch)),
                           writes=(f"ps{bank}",))
                    ctx.op("vector", lambda v, kc=kc, tch=tch, bank=bank: v.tensor_scalar(
                        out=outf(kc, tch), in0=ps[bank][:, :],
                        scalar1=bias[bname][:, kc:kc + 1], scalar2=None, op0=ALU.add),
                        reads=(f"ps{bank}", bname), writes=(ores(kc),))

        def load_w(name, rows, cols, scope):
            nch = rows // 128
            tiles = [sb(scope, nc, f"sw_{name}{c}", [128, cols], BF16) for c in range(nch)]
            for c in range(nch):
                ctx.dma("gpsimd", tiles[c][:, :], dr[name][128 * c:128 * (c + 1), :],
                        writes=(f"sw_{name}{c}",))
            return tiles

        with ExitStack() as kst:
            K_fm = [sb(kst, nc, f"K{c}", [128, SEQ], BF16) for c in range(16)]

            with ExitStack() as pst:
                xlnT = [sb(pst, nc, f"xlnT{c}", [128, SEQ], BF16, is_transpose_dest=True)
                        for c in range(16)]
                with ExitStack() as post:
                    xlnTo = [sb(post, nc, f"xlnTo{c}", [128, NT], BF16, is_transpose_dest=True)
                             for c in range(16)]
                    with ExitStack() as lst:
                        xldb = sb(lst, nc, "xldb", [128, D], F32)
                        xln = sb(lst, nc, "xln", [128, D], BF16)

                        def ln_tiles(src_ap, n_tiles, dstT, dres):
                            for i in range(n_tiles):
                                ctx.dma("gpsimd", xldb[:], src_ap[128 * i:128 * (i + 1), :],
                                        writes=("xldb",))
                                emit_ln_stats(ctx, st, 0, "xldb", xldb[:], xln[:], "xln")
                                ctx.op("scalar", lambda s: s.activation(
                                    out=xln[:], in_=xldb[:], func=AF.Identity,
                                    bias=st[:, 19:20], scale=st[:, 9:10]),
                                    reads=("xldb", "st0"), writes=("xln",))
                                for c in range(16):
                                    ctx.dma("sync", dstT[c][:, 128 * i:128 * (i + 1)],
                                            xln[:, 128 * c:128 * (c + 1)], transpose=True,
                                            reads=("xln",), writes=(f"{dres}{c}",))

                        ln_tiles(xm, 16, xlnT, "xlnT")
                        ln_tiles(xmo, 8, xlnTo, "xlnTo")
                    ctx.free_barrier(["xldb", "xln"])

                    with ExitStack() as ws:
                        wkv = load_w("w_kv", D, 512, ws)
                        proj(wkv, "sw_w_kv", 4, xlnT, "xlnT", SEQ,
                             lambda kc, tch: ckv[kc][:, 512 * tch:512 * (tch + 1)], "b_kv",
                             lambda kc: f"ckv{kc}")
                    ctx.free_barrier([f"sw_w_kv{c}" for c in range(16)])
                    with ExitStack() as ws:
                        wq_ = load_w("w_q", D, 512, ws)
                        proj(wq_, "sw_w_q", 4, xlnTo, "xlnTo", NT,
                             lambda kc, tch: qlat[kc][:, 512 * tch:512 * (tch + 1)], "b_q",
                             lambda kc: f"qlat{kc}")
                    ctx.free_barrier([f"sw_w_q{c}" for c in range(16)])
                ctx.free_barrier([f"xlnTo{c}" for c in range(16)])
                with ExitStack() as ws:
                    wkr = load_w("w_kr", D, 1024, ws)
                    proj(wkr, "sw_w_kr", 8, xlnT, "xlnT", SEQ,
                         lambda kc, tch: K_fm[8 + kc][:, 512 * tch:512 * (tch + 1)], "b_kr",
                         lambda kc: f"K{8 + kc}")
                ctx.free_barrier([f"sw_w_kr{c}" for c in range(16)])
            ctx.free_barrier([f"xlnT{c}" for c in range(16)])

            with ExitStack() as ws:
                wku = load_w("w_ku", 512, 1024, ws)
                proj(wku, "sw_w_ku", 8, ckv, "ckv", SEQ,
                     lambda kc, tch: K_fm[kc][:, 512 * tch:512 * (tch + 1)], "b_ku",
                     lambda kc: f"K{kc}")
            ctx.free_barrier([f"sw_w_ku{c}" for c in range(4)])

            with ExitStack() as ws:
                wvu = load_w("w_vu", 512, D, ws)
                vtmp = [sb(ws, nc, f"vtmp{i}", [128, SEQ], BF16) for i in range(2)]
                for vc in range(16):
                    par = vc % 2
                    for tch in range(4):
                        bank = tch % 2

                        def mmv(te, vc=vc, tch=tch, bank=bank):
                            ins = None
                            for ch in range(4):
                                ins = te.matmul(ps[bank][:, :],
                                                wvu[ch][:, 128 * vc:128 * (vc + 1)],
                                                ckv[ch][:, 512 * tch:512 * (tch + 1)],
                                                start=(ch == 0), stop=(ch == 3))
                            return ins
                        ctx.op("tensor", mmv,
                               reads=tuple(f"sw_w_vu{c}" for c in range(4)) +
                                     tuple(f"ckv{c}" for c in range(4)),
                               writes=(f"ps{bank}",))
                        ctx.op("vector", lambda v, vc=vc, tch=tch, bank=bank, par=par:
                               v.tensor_scalar(out=vtmp[par][:, 512 * tch:512 * (tch + 1)],
                                               in0=ps[bank][:, :],
                                               scalar1=bias["b_vu"][:, vc:vc + 1],
                                               scalar2=None, op0=ALU.add),
                               reads=(f"ps{bank}", "b_vu"), writes=(f"vtmp{par}",))
                    ctx.dma("gpsimd", vscr[128 * vc:128 * (vc + 1), :], vtmp[par][:, :],
                            reads=(f"vtmp{par}",), writes=("vscr",))
            ctx.free_barrier([f"sw_w_vu{c}" for c in range(4)] + ["vtmp0", "vtmp1"])
            ctx.free_barrier([f"ckv{c}" for c in range(4)])

            with ExitStack() as ws:
                wqu = load_w("w_qu", 512, 1024, ws)
                wqr = load_w("w_qr", 512, 1024, ws)
                qtmp = [sb(ws, nc, f"qtmp{i}", [128, NT], BF16) for i in range(2)]
                for qi in range(16):
                    par = qi % 2
                    wt = wqu if qi < 8 else wqr
                    wn = "sw_w_qu" if qi < 8 else "sw_w_qr"
                    bn = "b_qu" if qi < 8 else "b_qr"
                    kc = qi % 8
                    for tch in range(2):
                        bank = tch % 2

                        def mmq(te, kc=kc, tch=tch, bank=bank, wt=wt):
                            ins = None
                            for ch in range(4):
                                ins = te.matmul(ps[bank][:, :],
                                                wt[ch][:, 128 * kc:128 * (kc + 1)],
                                                qlat[ch][:, 512 * tch:512 * (tch + 1)],
                                                start=(ch == 0), stop=(ch == 3))
                            return ins
                        ctx.op("tensor", mmq,
                               reads=tuple(f"{wn}{c}" for c in range(4)) +
                                     tuple(f"qlat{c}" for c in range(4)),
                               writes=(f"ps{bank}",))
                        ctx.op("vector", lambda v, kc=kc, tch=tch, bank=bank, par=par, bn=bn:
                               v.tensor_scalar(out=qtmp[par][:, 512 * tch:512 * (tch + 1)],
                                               in0=ps[bank][:, :],
                                               scalar1=bias[bn][:, kc:kc + 1],
                                               scalar2=None, op0=ALU.add),
                               reads=(f"ps{bank}", bn), writes=(f"qtmp{par}",))
                    ctx.dma("gpsimd", qscr[128 * qi:128 * (qi + 1), :], qtmp[par][:, :],
                            reads=(f"qtmp{par}",), writes=("qscr",))
            ctx.free_barrier([f"sw_w_qu{c}" for c in range(4)] +
                             [f"sw_w_qr{c}" for c in range(4)] + ["qtmp0", "qtmp1"] +
                             [f"qlat{c}" for c in range(4)])

            with ExitStack() as ast:
                Qh = [sb(ast, nc, f"Qh{p}", [128, NT], BF16) for p in range(2)]
                Vh = [[sb(ast, nc, f"Vh{p}_{tc}", [128, 128], BF16, is_transpose_dest=True)
                       for tc in range(16)] for p in range(2)]
                Pb = sb(ast, nc, "Pb", [128, SEQ], BF16)
                PT = [sb(ast, nc, f"PT{tc}", [128, 128], BF16, is_transpose_dest=True)
                      for tc in range(16)]
                msk = [sb(ast, nc, f"msk{j}", [128, 512], F32) for j in range(8)]
                rs = sb(ast, nc, "rs", [128, 8], F32)
                ath = [sb(ast, nc, f"ath{p}", [128, NT], BF16) for p in range(2)]

                for j in range(8):
                    ctx.dma("gpsimd", msk[j][:, :], masks[j, :, :], writes=(f"msk{j}",))

                def load_head(h):
                    p = h % 2
                    ctx.dma("sync", Qh[p][:, :], qscr[128 * h:128 * (h + 1), :],
                            reads=("qscr",), writes=(f"Qh{p}",))
                    for tc in range(16):
                        ctx.dma("sync", Vh[p][tc][:, :],
                                vscr[128 * h:128 * (h + 1), 128 * tc:128 * (tc + 1)],
                                transpose=True, reads=("vscr",), writes=(f"Vh{p}_{tc}",))

                load_head(0)
                load_head(1)
                for h in range(16):
                    p = h % 2
                    for j in range(8):
                        e = EXT[j]
                        nkc = e // 512
                        for kc in range(nkc):
                            bank = kc % 2
                            ctx.op("tensor", lambda te, h=h, j=j, kc=kc, bank=bank, p=p:
                                   te.matmul(ps[bank][:, :], Qh[p][:, 128 * j:128 * (j + 1)],
                                             K_fm[h][:, 512 * kc:512 * (kc + 1)],
                                             start=True, stop=True),
                                   reads=(f"Qh{p}", f"K{h}"), writes=(f"ps{bank}",))
                            if kc == nkc - 1:
                                ctx.op("vector", lambda v, j=j, bank=bank: v.tensor_tensor(
                                    out=ps[bank][:, :], in0=ps[bank][:, :], in1=msk[j][:, :],
                                    op=ALU.add),
                                    reads=(f"msk{j}", f"ps{bank}"), writes=(f"ps{bank}",))
                            ctx.op("scalar", lambda s, kc=kc, bank=bank: s.activation(
                                out=Pb[:, 512 * kc:512 * (kc + 1)], in_=ps[bank][:, :],
                                func=AF.Exp, scale=SCL, accum_out=rs[:, kc:kc + 1]),
                                reads=(f"ps{bank}",), writes=("Pb", "rs"))
                        ctx.op("vector", lambda v, nkc=nkc: v.reduce_sum(
                            out=rs[:, 6:7], in_=rs[:, 0:nkc], axis=AX),
                            reads=("rs",), writes=("rs",))
                        ctx.op("vector", lambda v: v.reciprocal(out=rs[:, 7:8], in_=rs[:, 6:7]),
                               reads=("rs",), writes=("rs",))
                        ctx.op("vector", lambda v, e=e: v.tensor_scalar(
                            out=Pb[:, 0:e], in0=Pb[:, 0:e], scalar1=rs[:, 7:8], scalar2=None,
                            op0=ALU.mult),
                            reads=("rs", "Pb"), writes=("Pb",))
                        for tc in range(e // 128):
                            ctx.dma("sync", PT[tc][:, :], Pb[:, 128 * tc:128 * (tc + 1)],
                                    transpose=True, reads=("Pb",), writes=(f"PT{tc}",))

                        def mav(te, e=e, p=p):
                            ins = None
                            for tc in range(e // 128):
                                ins = te.matmul(ps[4][:, 0:128], Vh[p][tc][:, :], PT[tc][:, :],
                                                start=(tc == 0), stop=(tc == e // 128 - 1))
                            return ins
                        ctx.op("tensor", mav,
                               reads=tuple(f"Vh{p}_{tc}" for tc in range(e // 128)) +
                                     tuple(f"PT{tc}" for tc in range(e // 128)),
                               writes=("ps4",))
                        ctx.op("scalar", lambda s, j=j, p=p: s.activation(
                            out=ath[p][:, 128 * j:128 * (j + 1)], in_=ps[4][:, 0:128],
                            func=AF.Identity, scale=1.0),
                            reads=("ps4",), writes=(f"ath{p}",))
                    ctx.dma("gpsimd", ascr[128 * h:128 * (h + 1), :], ath[p][:, :],
                            reads=(f"ath{p}",), writes=("ascr",))
                    if h + 2 < 16:
                        load_head(h + 2)
        ctx.free_barrier([f"K{c}" for c in range(16)])

        with ExitStack() as ost:
            wo = [sb(ost, nc, f"wo{a}", [128, D], BF16) for a in range(16)]
            atl = [sb(ost, nc, f"atl{a}", [128, 256], BF16) for a in range(16)]
            xmt = [sb(ost, nc, f"xmt{i}", [128, D], F32) for i in range(2)]
            obuf = [sb(ost, nc, f"obufo{i}", [128, 512], F32) for i in range(2)]
            b_o_bc = sb(ost, nc, "b_o_bc", [128, D], F32)
            ctx.dma("gpsimd", b_o_bc[:, :],
                    dr["b_o"].rearrange("(a d) -> a d", a=1).broadcast_to((128, D)),
                    writes=("b_o_bc",))
            for a in range(16):
                ctx.dma("gpsimd", wo[a][:, :], dr["w_o"][128 * a:128 * (a + 1), :],
                        writes=(f"wo{a}",))
            for jp in range(4):
                for jl in range(2):
                    j = 2 * jp + jl
                    ctx.dma("gpsimd", xmt[jl][:], xmo[128 * j:128 * (j + 1), :],
                            writes=(f"xmt{jl}",))
                    ctx.op("vector", lambda v, jl=jl: v.tensor_tensor(
                        out=xmt[jl][:], in0=xmt[jl][:], in1=b_o_bc[:], op=ALU.add),
                        reads=("b_o_bc",), writes=(f"xmt{jl}",))
                for a in range(16):
                    ctx.dma("sync", atl[a][:, :], ascr[128 * a:128 * (a + 1), 256 * jp:256 * (jp + 1)],
                            reads=("ascr",), writes=(f"atl{a}",))

                def mmo(te, jp=jp):
                    ins = None
                    for a in range(16):
                        for jl in range(2):
                            for dc in range(4):
                                ins = te.matmul(ps[jl * 4 + dc][:, :],
                                                atl[a][:, 128 * jl:128 * (jl + 1)],
                                                wo[a][:, 512 * dc:512 * (dc + 1)],
                                                start=(a == 0), stop=(a == 15))
                    return ins
                ctx.op("tensor", mmo,
                       reads=tuple(f"atl{a}" for a in range(16)) +
                             tuple(f"wo{a}" for a in range(16)),
                       writes=tuple(f"ps{i}" for i in range(8)))
                for jl in range(2):
                    for dc in range(4):
                        ob = (jl * 4 + dc) % 2
                        ctx.op("scalar", lambda s, jl=jl, dc=dc, ob=ob: s.activation(
                            out=obuf[ob][:], in_=ps[jl * 4 + dc][:, :], func=AF.Identity,
                            scale=1.0),
                            reads=(f"ps{jl * 4 + dc}",), writes=(f"obufo{ob}",))
                        ctx.op("vector", lambda v, jl=jl, dc=dc, ob=ob: v.tensor_tensor(
                            out=xmt[jl][:, 512 * dc:512 * (dc + 1)],
                            in0=xmt[jl][:, 512 * dc:512 * (dc + 1)], in1=obuf[ob][:],
                            op=ALU.add),
                            reads=(f"obufo{ob}",), writes=(f"xmt{jl}",))
                for jl in range(2):
                    j = 2 * jp + jl
                    ctx.dma("gpsimd", xout[128 * j:128 * (j + 1), :], xmt[jl][:],
                            reads=(f"xmt{jl}",))

        ctx.finish("gpsimd")
        ctx.flush()
    return nc


def emit_ffn(ctx, nc, tag, xt, st, sc_bc, xq_tmp, w1q, w3q, w2q, wres):
    T = tag
    W1R = HID // 8
    with ExitStack() as fst:
        ps = [fst.enter_context(nc.psum_tensor(f"{T}ps{i}", [128, 512], F32))
              for i in range(8)]
        xqT = sb(fst, nc, f"{T}xqT", [128, 16 * 512], BF16, is_transpose_dest=True)
        hb = [sb(fst, nc, f"{T}hb{k}", [128, HID], BF16) for k in range(4)]
        tmp512 = sb(fst, nc, f"{T}tmp512", [128, 512], F32)
        s1b = [sb(fst, nc, f"{T}s1b{i}", [128, 512], F32) for i in range(2)]
        s3b = [sb(fst, nc, f"{T}s3b{i}", [128, 512], F32) for i in range(2)]
        hmx = sb(fst, nc, f"{T}hmx", [128, 64], F32)

        def c0_of(g, k):
            return 24 * (4 * (g % 2) + k)

        def emit_ln_group(g):
            for k in range(4):
                t = 4 * g + k
                c0 = c0_of(g, k)
                stres = f"st{c0}"
                emit_ln_stats(ctx, st, c0, f"xt{t}", xt[t][:], xq_tmp[:], "xq_tmp")
                ctx.op("vector", lambda v, c0=c0: v.tensor_tensor(
                    out=st[:, c0 + 17:c0 + 18], in0=st[:, c0 + 13:c0 + 14],
                    in1=sc_bc[:, 3:4], op=ALU.mult),
                    reads=(stres, "sc_bc"), writes=(stres,))
                ctx.op("vector", lambda v, c0=c0: v.tensor_tensor(
                    out=st[:, c0 + 18:c0 + 19], in0=st[:, c0 + 13:c0 + 14],
                    in1=sc_bc[:, 4:5], op=ALU.mult),
                    reads=(stres, "sc_bc"), writes=(stres,))
                for q4 in range(4):
                    sl = slice(512 * q4, 512 * (q4 + 1))
                    ctx.op("vector", lambda v, t=t, c0=c0, sl=sl: v.tensor_scalar(
                        out=tmp512[:], in0=xt[t][:, sl], scalar1=st[:, c0 + 15:c0 + 16],
                        scalar2=st[:, c0 + 16:c0 + 17], op0=ALU.mult, op1=ALU.add),
                        reads=(f"xt{t}", stres), writes=(f"{T}tmp512",))
                    ctx.op("vector", lambda v, sl=sl: v.tensor_scalar(
                        out=xq_tmp[:, sl], in0=tmp512[:], scalar1=MAGIC, scalar2=MAGIC,
                        op0=ALU.add, op1=ALU.subtract),
                        reads=(f"{T}tmp512",), writes=("xq_tmp",))
                for c in range(16):
                    ctx.dma("sync" if c % 2 == 0 else "scalar",
                            xqT[:, 512 * c + 128 * k:512 * c + 128 * (k + 1)],
                            xq_tmp[:, 128 * c:128 * (c + 1)], transpose=True,
                            reads=("xq_tmp",), writes=(f"{T}xqTk{k}",))

        def emit_hquant(g, k):
            c0 = c0_of(g, k)
            stres = f"st{c0}"
            ctx.op("vector", lambda v, k=k, c0=c0: v.reduce_max(
                out=st[:, c0 + 20:c0 + 21], in_=hmx[:, 16 * k:16 * k + HCH], axis=AX),
                reads=(f"{T}hmx",), writes=(stres,))
            ctx.op("vector", lambda v, c0=c0: v.tensor_scalar(
                out=st[:, c0 + 20:c0 + 21], in0=st[:, c0 + 20:c0 + 21],
                scalar1=EPS, scalar2=None, op0=ALU.max),
                reads=(stres,), writes=(stres,))
            ctx.op("vector", lambda v, c0=c0: v.reciprocal(
                out=st[:, c0 + 21:c0 + 22], in_=st[:, c0 + 20:c0 + 21]),
                reads=(stres,), writes=(stres,))
            ctx.op("vector", lambda v, c0=c0: v.tensor_scalar(
                out=st[:, c0 + 21:c0 + 22], in0=st[:, c0 + 21:c0 + 22],
                scalar1=127.0, scalar2=None, op0=ALU.mult),
                reads=(stres,), writes=(stres,))
            ctx.op("vector", lambda v, c0=c0: v.tensor_tensor(
                out=st[:, c0 + 22:c0 + 23], in0=st[:, c0 + 20:c0 + 21],
                in1=sc_bc[:, 5:6], op=ALU.mult),
                reads=(stres, "sc_bc"), writes=(stres,))
            for hc in range(HCH):
                ctx.op("vector", lambda v, k=k, hc=hc, c0=c0: v.tensor_scalar(
                    out=tmp512[:], in0=hb[k][:, 512 * hc:512 * (hc + 1)],
                    scalar1=st[:, c0 + 21:c0 + 22], scalar2=MAGIC,
                    op0=ALU.mult, op1=ALU.add),
                    reads=(f"{T}hb{k}", stres), writes=(f"{T}tmp512",))
                ctx.op("vector", lambda v, k=k, hc=hc: v.tensor_scalar(
                    out=hb[k][:, 512 * hc:512 * (hc + 1)], in0=tmp512[:],
                    scalar1=MAGIC, scalar2=None, op0=ALU.subtract),
                    reads=(f"{T}tmp512",), writes=(f"{T}hb{k}",))

        emit_ln_group(0)
        for g in range(2):
            with ExitStack() as wsc:
                w1T = [sb(wsc, nc, f"{T}w1T{i}_g{g}", [128, 16 * 512], BF16)
                       for i in range(2)]
                w3T = [sb(wsc, nc, f"{T}w3T{i}_g{g}", [128, 16 * 512], BF16)
                       for i in range(2)]

                def load_w13(hc):
                    par = hc % 2
                    lo, hi = 512 * hc, 512 * (hc + 1)
                    for dstt, wq_, res, bres in ((w1T, w1q, wres[0], f"{T}w1T{par}"),
                                                 (w3T, w3q, wres[1], f"{T}w3T{par}")):
                        pos = lo
                        while pos < hi:
                            r = pos // W1R
                            b = min(hi, (r + 1) * W1R)
                            w = b - pos
                            src = wq_[r, :, pos - r * W1R:pos - r * W1R + w]
                            src3 = src.rearrange("(c p) k -> p c k", p=128)
                            dst3 = dstt[par][:, :].rearrange("p (c k) -> p c k", k=512)
                            off = pos - lo
                            ctx.dma("sync", dst3[:, :, off:off + w], src3,
                                    reads=(res,), writes=(bres,))
                            pos = b

                load_w13(0)
                load_w13(1)
                for hc in range(HCH):
                    par = hc % 2
                    for k in range(4):
                        b1, b3 = 2 * (k % 2), 2 * (k % 2) + 1

                        def mms(te, k=k, par=par, b1=b1, b3=b3):
                            ins = None
                            for c in range(16):
                                lhs = xqT[:, 512 * c + 128 * k:512 * c + 128 * (k + 1)]
                                te.matmul(ps[b1][:, :], lhs, w1T[par][:, 512 * c:512 * (c + 1)],
                                          start=(c == 0), stop=(c == 15))
                                ins = te.matmul(ps[b3][:, :], lhs,
                                                w3T[par][:, 512 * c:512 * (c + 1)],
                                                start=(c == 0), stop=(c == 15))
                            return ins
                        ctx.op("tensor", mms,
                               reads=(f"{T}xqTk{k}", f"{T}w1T{par}", f"{T}w3T{par}"),
                               writes=(f"{T}ps{b1}", f"{T}ps{b3}"))
                        c0 = c0_of(g, k)
                        sp = (hc * 4 + k) % 2
                        ctx.op("scalar", lambda s, b1=b1, c0=c0, sp=sp: s.activation(
                            out=s1b[sp][:], in_=ps[b1][:, :], func=AF.Silu,
                            scale=st[:, c0 + 17:c0 + 18]),
                            reads=(f"{T}ps{b1}", f"st{c0}"), writes=(f"{T}s1b{sp}",))
                        ctx.op("vector", lambda v, b3=b3, c0=c0, sp=sp: v.tensor_scalar(
                            out=s3b[sp][:], in0=ps[b3][:, :],
                            scalar1=st[:, c0 + 18:c0 + 19], scalar2=None, op0=ALU.mult),
                            reads=(f"{T}ps{b3}", f"st{c0}"), writes=(f"{T}s3b{sp}",))
                        ctx.op("vector", lambda v, k=k, hc=hc, sp=sp: v.tensor_tensor(
                            out=hb[k][:, 512 * hc:512 * (hc + 1)], in0=s1b[sp][:],
                            in1=s3b[sp][:], op=ALU.mult),
                            reads=(f"{T}s1b{sp}", f"{T}s3b{sp}"), writes=(f"{T}hb{k}",))
                        ctx.op("vector", lambda v, k=k, hc=hc: v.reduce_max(
                            out=hmx[:, 16 * k + hc:16 * k + hc + 1],
                            in_=hb[k][:, 512 * hc:512 * (hc + 1)], axis=AX,
                            apply_absolute_value=True),
                            reads=(f"{T}hb{k}",), writes=(f"{T}hmx",))
                        if hc == HCH - 1:
                            emit_hquant(g, k)
                    if hc + 2 < HCH:
                        load_w13(hc + 2)
                ctx.free_barrier([f"{T}w1T0", f"{T}w1T1", f"{T}w3T0", f"{T}w3T1"])

            with ExitStack() as w2s:
                hqT = sb(w2s, nc, f"{T}hqT_g{g}", [128, 44 * 512], BF16,
                         is_transpose_dest=True)
                w2b = [sb(w2s, nc, f"{T}w2b{i}_g{g}", [128, 1024], BF16) for i in range(2)]

                def load_w2d(hch, dcp):
                    ctx.dma("sync", w2b[hch % 2][:, :],
                            w2q[128 * hch:128 * (hch + 1), 1024 * dcp:1024 * (dcp + 1)],
                            reads=(wres[2],), writes=(f"{T}w2b{hch % 2}",))

                load_w2d(0, 0)
                load_w2d(1, 0)
                for c in range(44):
                    for k in range(4):
                        ctx.dma("sync" if (c * 4 + k) % 2 == 0 else "scalar",
                                hqT[:, 512 * c + 128 * k:512 * c + 128 * (k + 1)],
                                hb[k][:, 128 * c:128 * (c + 1)], transpose=True,
                                reads=(f"{T}hb{k}",), writes=(f"{T}hqTc{c}",))

                for dcp in range(2):
                    def load_w2(hch, dcp=dcp):
                        load_w2d(hch, dcp)
                    if dcp == 1:
                        load_w2(0)
                        load_w2(1)
                    for hch in range(44):
                        def mm2(te, hch=hch):
                            ins = None
                            for k in range(4):
                                lhs = hqT[:, 512 * hch + 128 * k:512 * hch + 128 * (k + 1)]
                                for dci in range(2):
                                    ins = te.matmul(ps[k * 2 + dci][:, :], lhs,
                                                    w2b[hch % 2][:, 512 * dci:512 * (dci + 1)],
                                                    start=(hch == 0), stop=(hch == 43))
                            return ins
                        ctx.op("tensor", mm2,
                               reads=(f"{T}hqTc{hch}", f"{T}w2b{hch % 2}"),
                               writes=tuple(f"{T}ps{i}" for i in range(8)))
                        if hch + 2 < 44:
                            load_w2(hch + 2)
                    for k in range(4):
                        t = 4 * g + k
                        c0 = c0_of(g, k)
                        for dci in range(2):
                            lo = 1024 * dcp + 512 * dci
                            ctx.op("vector", lambda v, k=k, dci=dci, t=t, lo=lo, c0=c0:
                                   v.scalar_tensor_tensor(
                                       out=xt[t][:, lo:lo + 512], in0=ps[k * 2 + dci][:, :],
                                       scalar=st[:, c0 + 22:c0 + 23],
                                       in1=xt[t][:, lo:lo + 512], op0=ALU.mult, op1=ALU.add),
                                   reads=(f"{T}ps{k * 2 + dci}", f"st{c0}"), writes=(f"xt{t}",))
                    if dcp == 0 and g == 0:
                        emit_ln_group(1)
                ctx.free_barrier([f"{T}w2b0", f"{T}w2b1"] +
                                 [f"{T}hqTc{c}" for c in range(44)])
        ctx.free_barrier([f"{T}xqT"] + [f"{T}xqTk{k}" for k in range(4)] +
                         [f"{T}hb{k}" for k in range(4)] +
                         [f"{T}tmp512", f"{T}s1b0", f"{T}s1b1", f"{T}s3b0", f"{T}s3b1",
                          f"{T}hmx"] + [f"{T}ps{i}" for i in range(8)])


def build_fused():
    nc = bass.Bass("TRN2", target_bir_lowering=False)
    x = nc.dram_tensor("x", [NT, D], F32, kind="ExternalInput").ap()
    xo = nc.dram_tensor("xo", [NT, D], F32, kind="ExternalOutput").ap()
    W1R = HID // 8
    w1s = nc.dram_tensor("w1s", [W1R, D], F32, kind="ExternalInput").ap()
    w3s = nc.dram_tensor("w3s", [W1R, D], F32, kind="ExternalInput").ap()
    w2s = nc.dram_tensor("w2s", [D, W1R], F32, kind="ExternalInput").ap()
    ident = nc.dram_tensor("ident", [128, 128], BF16, kind="ExternalInput").ap()
    masks = nc.dram_tensor("masks", [8, 128, 512], F32, kind="ExternalInput").ap()
    shp = {"w_kv": [D, 512], "b_kv": [512], "w_ku": [512, 1024], "b_ku": [1024],
           "w_kr": [D, 1024], "b_kr": [1024], "w_vu": [512, D], "b_vu": [D],
           "w_q": [D, 512], "b_q": [512], "w_qu": [512, 1024], "b_qu": [1024],
           "w_qr": [512, 1024], "b_qr": [1024], "w_o": [D, D], "b_o": [D]}
    dr = {n: nc.dram_tensor(n, s, F32, kind="ExternalInput").ap() for n, s in shp.items()}
    w1qs = nc.dram_tensor("w1qs", [D, W1R], BF16).ap()
    w3qs = nc.dram_tensor("w3qs", [D, W1R], BF16).ap()
    w2qs = nc.dram_tensor("w2qs", [W1R, D], BF16).ap()
    w1qd = nc.dram_tensor("w1qd", [8, D, W1R], BF16).ap()
    w3qd = nc.dram_tensor("w3qd", [8, D, W1R], BF16).ap()
    w2qd = nc.dram_tensor("w2qd", [HID, D], BF16).ap()
    scp = nc.dram_tensor("scp", [1, 4], F32).ap()
    scs = nc.dram_tensor("scs", [1, 4], F32).ap()
    scb = nc.dram_tensor("scb", [1, 8], F32).ap()
    kvb = nc.dram_tensor("kvb", [2, D, NT], BF16).ap()
    kvg = nc.dram_tensor("kvg", [2, 2, D, NT], BF16).ap()
    qscr = nc.dram_tensor("qscr", [D, NT], BF16).ap()
    ascr = nc.dram_tensor("ascr", [D, NT], BF16).ap()
    G8 = [list(range(8))]
    GP = [[0, 1], [2, 3], [4, 5], [6, 7]]

    with ExitStack() as stack:
        block = stack.enter_context(nc.Block())
        ctx = Ctx(nc, block, stack)

        xt = [sb(stack, nc, f"xt{t}", [128, D], F32) for t in range(8)]
        st = sb(stack, nc, "st", [128, 8 * 24], F32)
        sc_bc = sb(stack, nc, "sc_bc", [128, 8], F32)
        xq_tmp = sb(stack, nc, "xq_tmp", [128, D], BF16)
        ones = sb(stack, nc, "ones", [128, 1], F32)
        ctx.op("vector", lambda v: v.memset(ones[:], 1.0), writes=("ones",))
        ctx.epsc = sb(stack, nc, "epsc", [128, 1], F32)
        ctx.op("vector", lambda v: v.memset(ctx.epsc[:], EPS), writes=("epsc",))
        bias = {}
        for n, s in shp.items():
            if n.startswith("b_") and n != "b_o":
                nch = s[0] // 128
                bias[n] = sb(stack, nc, f"sb{n}", [128, nch], F32)
                bv = dr[n].rearrange("(c p) -> c p", p=128)
                for c in range(nch):
                    ctx.dma("gpsimd", bias[n][:, c:c + 1], bv[c:c + 1, :].rearrange("a p -> p a"),
                            writes=(n,))
        for t in range(8):
            ctx.dma("gpsimd", xt[t][:], x[128 * t:128 * (t + 1), :], writes=(f"xt{t}",))

        with ExitStack() as wst:
            wld = [sb(wst, nc, f"wld{i}", [128, D], F32) for i in range(2)]
            wqt = [sb(wst, nc, f"wqt{i}", [128, D], BF16) for i in range(2)]
            p1b = sb(wst, nc, "p1b", [128, D], F32)
            acc = sb(wst, nc, "acc", [128, 8], F32)
            sc_sm = sb(wst, nc, "sc_sm", [1, 8], F32)
            stage = sb(wst, nc, "stage", [128, 12288], BF16, is_transpose_dest=True)
            psw = wst.enter_context(nc.psum_tensor("psw", [128, 512], F32))

            ctx.op("vector", lambda v: v.memset(acc[:], 0.0), writes=("acc",))
            ctx.op("vector", lambda v: v.memset(sc_sm[:], 0.0), writes=("sc_sm",))
            ANCH = [0, 128, 256, 384, 512, 576]
            shard = [(0, w1s, W1R, D), (1, w3s, W1R, D), (2, w2s, D, W1R)]

            for wi, src, R, C in shard:
                r = 0
                i = 0
                while r < R:
                    nr = min(128, R - r)
                    par = 0
                    ctx.dma("sync", wld[par][0:nr, 0:C], src[r:r + nr, :],
                            writes=(f"wld{par}",))
                    ctx.op("vector", lambda v, par=par, i=i, C=C, nr=nr, wi=wi: v.reduce_sum(
                        out=acc[0:nr, wi:wi + 1], in_=wld[par][0:nr, 0:C], axis=AX,
                        apply_absolute_value=True) if i == 0 else v.reduce_sum(
                        out=acc[0:nr, 4 + wi:5 + wi], in_=wld[par][0:nr, 0:C], axis=AX,
                        apply_absolute_value=True),
                        reads=(f"wld{par}",), writes=("acc",))
                    if i > 0:
                        ctx.op("vector", lambda v, wi=wi, nr=nr: v.tensor_tensor(
                            out=acc[0:nr, wi:wi + 1], in0=acc[0:nr, wi:wi + 1],
                            in1=acc[0:nr, 4 + wi:5 + wi], op=ALU.add),
                            reads=("acc",), writes=("acc",))
                    r += nr
                    i += 1
                ctx.op("tensor", lambda t, wi=wi: t.matmul(
                    psw[0:1, wi:wi + 1], ones[:], acc[:, wi:wi + 1], start=True, stop=True),
                    reads=("ones", "acc"), writes=("psw",))
                ctx.op("scalar", lambda s, wi=wi: s.copy(sc_sm[0:1, wi:wi + 1],
                                                         psw[0:1, wi:wi + 1]),
                       reads=("psw",), writes=("sc_sm",))
            ctx.dma("gpsimd", scp[:], sc_sm[0:1, 0:4], reads=("sc_sm",), writes=("scp",))
            ctx.collective("AllReduce", ALU.add, G8, [scp[:]], [scs[:]],
                           reads=("scp",), writes=("scs",))
            ctx.dma("gpsimd", sc_sm[0:1, 0:4], scs[:], reads=("scs",), writes=("sc_sm",))
            for wi, numel in [(0, HID * D), (1, HID * D), (2, D * HID)]:
                ctx.op("scalar", lambda s, wi=wi, numel=float(numel): s.mul(
                    sc_sm[0:1, wi:wi + 1], sc_sm[0:1, wi:wi + 1], 1.0 / numel),
                    reads=("sc_sm",), writes=("sc_sm",))
            ctx.op("vector", lambda v: v.tensor_scalar(out=sc_sm[0:1, 0:3], in0=sc_sm[0:1, 0:3],
                                                       scalar1=EPS, scalar2=None, op0=ALU.max),
                   reads=("sc_sm",), writes=("sc_sm",))
            ctx.op("vector", lambda v: v.reciprocal(out=sc_sm[0:1, 3:6], in_=sc_sm[0:1, 0:3]),
                   reads=("sc_sm",), writes=("sc_sm",))
            ctx.dma("gpsimd", scb[:], sc_sm[0:1, :], reads=("sc_sm",), writes=("scb",))
            ctx.dma("gpsimd", sc_bc[:, 0:3], scb[0:1, 3:6].broadcast_to((128, 3)),
                    reads=("scb",), writes=("sc_bc",))
            ctx.dma("gpsimd", sc_bc[:, 3:6], scb[0:1, 0:3].broadcast_to((128, 3)),
                    reads=("scb",), writes=("sc_bc",))
            ctx.op("vector", lambda v: v.tensor_scalar(out=sc_bc[:, 3:5], in0=sc_bc[:, 3:5],
                                                       scalar1=1.0 / 127.0, scalar2=None,
                                                       op0=ALU.mult),
                   reads=("sc_bc",), writes=("sc_bc",))
            ctx.op("vector", lambda v: v.tensor_scalar(out=sc_bc[:, 5:6], in0=sc_bc[:, 5:6],
                                                       scalar1=0.5 / 127.0, scalar2=None,
                                                       op0=ALU.mult),
                   reads=("sc_bc",), writes=("sc_bc",))

            for wi, src, R, C in shard:
                dst = [w1qs, w3qs, w2qs][wi]
                anchors = ANCH if R == W1R else [128 * i for i in range(16)]
                canch = ANCH if C == W1R else [128 * i for i in range(C // 128)]
                nst = len(canch)
                for i, r0 in enumerate(anchors):
                    par = 0
                    ctx.dma("sync", wld[par][:, 0:C], src[r0:r0 + 128, :],
                            writes=(f"wld{par}",))
                    ctx.op("vector", lambda v, par=par, C=C, wi=wi: v.tensor_scalar(
                        out=p1b[:, 0:C], in0=wld[par][:, 0:C], scalar1=sc_bc[:, wi:wi + 1],
                        scalar2=MAGIC, op0=ALU.mult, op1=ALU.add),
                        reads=(f"wld{par}", "sc_bc"), writes=("p1b",))
                    ctx.op("vector", lambda v, C=C: v.tensor_scalar(
                        out=p1b[:, 0:C], in0=p1b[:, 0:C], scalar1=MAGIC, scalar2=1.0,
                        op0=ALU.subtract, op1=ALU.min),
                        reads=("p1b",), writes=("p1b",))
                    ctx.op("vector", lambda v, par=par, C=C: v.tensor_scalar(
                        out=wqt[par][:, 0:C], in0=p1b[:, 0:C], scalar1=-1.0, scalar2=None,
                        op0=ALU.max),
                        reads=("p1b",), writes=(f"wqt{par}",))
                    for j, a in enumerate(canch):
                        q = "sync" if j % 2 == 0 else "scalar"
                        ctx.dma(q, stage[:, (R * j + r0):(R * j + r0 + 128)],
                                wqt[par][:, a:a + 128], transpose=True,
                                reads=(f"wqt{par}",), writes=("stage",))
                for j, a in enumerate(canch):
                    ctx.dma("gpsimd", dst[a:a + 128, :], stage[:, R * j:R * j + R],
                            reads=("stage",), writes=(f"wqs{wi}",))
                ctx.free_barrier(["stage"])
            ctx.collective("AllGather", ALU.bypass, G8, [w1qs[:]], [w1qd[:]],
                           reads=("wqs0",), writes=("w1qd",))
            ctx.collective("AllGather", ALU.bypass, G8, [w3qs[:]], [w3qd[:]],
                           reads=("wqs1",), writes=("w3qd",))
            ctx.collective("AllGather", ALU.bypass, G8, [w2qs[:]], [w2qd[:]],
                           reads=("wqs2",), writes=("w2qd",))
        ctx.free_barrier(["wld0", "wld1", "wqt0", "wqt1", "p1b", "acc", "sc_sm",
                          "stage", "psw"])

        emit_ffn(ctx, nc, "f1", xt, st, sc_bc, xq_tmp, w1qd, w3qd, w2qd,
                 ("w1qd", "w3qd", "w2qd"))

        def proj(wtiles, wres, nkc, rhs, rhs_res, rhs_len, outf, bname, ores):
            nch = len(wtiles)
            for kc in range(nkc):
                for tch in range(rhs_len // 512):
                    bank = (kc * (rhs_len // 512) + tch) % 2

                    def mmp(te, kc=kc, tch=tch, bank=bank):
                        ins = None
                        for ch in range(nch):
                            ins = te.matmul(ps[bank][:, :],
                                            wtiles[ch][:, 128 * kc:128 * (kc + 1)],
                                            rhs[ch][:, 512 * tch:512 * (tch + 1)],
                                            start=(ch == 0), stop=(ch == nch - 1))
                        return ins
                    ctx.op("tensor", mmp,
                           reads=tuple(f"{wres}{ch}" for ch in range(nch)) +
                                 tuple(f"{rhs_res}{ch}" for ch in range(nch)),
                           writes=(f"pjps{bank}",))
                    ctx.op("vector", lambda v, kc=kc, tch=tch, bank=bank: v.tensor_scalar(
                        out=outf(kc, tch), in0=ps[bank][:, :],
                        scalar1=bias[bname][:, kc:kc + 1], scalar2=None, op0=ALU.add),
                        reads=(f"pjps{bank}", bname), writes=(ores(kc),))

        def load_w(name, rows, cols, scope):
            nch = rows // 128
            tiles = [sb(scope, nc, f"sw_{name}{c}", [128, cols], BF16) for c in range(nch)]
            for c in range(nch):
                ctx.dma("gpsimd", tiles[c][:, :], dr[name][128 * c:128 * (c + 1), :],
                        writes=(f"sw_{name}{c}",))
            return tiles

        with ExitStack() as pst:
            ps = [pst.enter_context(nc.psum_tensor(f"pjps{i}", [128, 512], F32))
                  for i in range(2)]
            ckv = [sb(pst, nc, f"ckv{c}", [128, NT], BF16) for c in range(4)]
            qlat = [sb(pst, nc, f"qlat{c}", [128, NT], BF16) for c in range(4)]
            xlnTo = [sb(pst, nc, f"xlnTo{c}", [128, NT], BF16, is_transpose_dest=True)
                     for c in range(16)]
            K_own = [sb(pst, nc, f"Ko{c}", [128, NT], BF16) for c in range(16)]
            with ExitStack() as lst:
                xln = sb(lst, nc, "xln", [128, D], BF16)
                for i in range(8):
                    emit_ln_stats(ctx, st, 0, f"xt{i}", xt[i][:], xln[:], "xln")
                    ctx.op("scalar", lambda s, i=i: s.activation(
                        out=xln[:], in_=xt[i][:], func=AF.Identity,
                        bias=st[:, 19:20], scale=st[:, 9:10]),
                        reads=(f"xt{i}", "st0"), writes=("xln",))
                    for c in range(16):
                        ctx.dma("sync" if c % 2 == 0 else "scalar",
                                xlnTo[c][:, 128 * i:128 * (i + 1)],
                                xln[:, 128 * c:128 * (c + 1)], transpose=True,
                                reads=("xln",), writes=(f"xlnTo{c}",))
            ctx.free_barrier(["xln"])

            with ExitStack() as ws:
                wkv = load_w("w_kv", D, 512, ws)
                proj(wkv, "sw_w_kv", 4, xlnTo, "xlnTo", NT,
                     lambda kc, tch: ckv[kc][:, 512 * tch:512 * (tch + 1)], "b_kv",
                     lambda kc: f"ckv{kc}")
            ctx.free_barrier([f"sw_w_kv{c}" for c in range(16)])
            with ExitStack() as ws:
                wq_ = load_w("w_q", D, 512, ws)
                proj(wq_, "sw_w_q", 4, xlnTo, "xlnTo", NT,
                     lambda kc, tch: qlat[kc][:, 512 * tch:512 * (tch + 1)], "b_q",
                     lambda kc: f"qlat{kc}")
            ctx.free_barrier([f"sw_w_q{c}" for c in range(16)])
            with ExitStack() as ws:
                wkr = load_w("w_kr", D, 1024, ws)
                proj(wkr, "sw_w_kr", 8, xlnTo, "xlnTo", NT,
                     lambda kc, tch: K_own[8 + kc][:, 512 * tch:512 * (tch + 1)], "b_kr",
                     lambda kc: f"Ko{8 + kc}")
            ctx.free_barrier([f"sw_w_kr{c}" for c in range(16)])
            ctx.free_barrier([f"xlnTo{c}" for c in range(16)])

            with ExitStack() as ws:
                wku = load_w("w_ku", 512, 1024, ws)
                proj(wku, "sw_w_ku", 8, ckv, "ckv", NT,
                     lambda kc, tch: K_own[kc][:, 512 * tch:512 * (tch + 1)], "b_ku",
                     lambda kc: f"Ko{kc}")
            ctx.free_barrier([f"sw_w_ku{c}" for c in range(4)])
            for c in range(16):
                ctx.dma("gpsimd", kvb[0, 128 * c:128 * (c + 1), :], K_own[c][:, :],
                        reads=(f"Ko{c}",), writes=("kvb",))

            with ExitStack() as ws:
                wvu = load_w("w_vu", 512, D, ws)
                vtmp = [sb(ws, nc, f"vtmp{i}", [128, NT], BF16) for i in range(2)]
                for vc in range(16):
                    par = vc % 2
                    for tch in range(2):
                        bank = tch % 2

                        def mmv(te, vc=vc, tch=tch, bank=bank):
                            ins = None
                            for ch in range(4):
                                ins = te.matmul(ps[bank][:, :],
                                                wvu[ch][:, 128 * vc:128 * (vc + 1)],
                                                ckv[ch][:, 512 * tch:512 * (tch + 1)],
                                                start=(ch == 0), stop=(ch == 3))
                            return ins
                        ctx.op("tensor", mmv,
                               reads=tuple(f"sw_w_vu{c}" for c in range(4)) +
                                     tuple(f"ckv{c}" for c in range(4)),
                               writes=(f"pjps{bank}",))
                        ctx.op("vector", lambda v, vc=vc, tch=tch, bank=bank, par=par:
                               v.tensor_scalar(out=vtmp[par][:, 512 * tch:512 * (tch + 1)],
                                               in0=ps[bank][:, :],
                                               scalar1=bias["b_vu"][:, vc:vc + 1],
                                               scalar2=None, op0=ALU.add),
                               reads=(f"pjps{bank}", "b_vu"), writes=(f"vtmp{par}",))
                    ctx.dma("gpsimd", kvb[1, 128 * vc:128 * (vc + 1), :], vtmp[par][:, :],
                            reads=(f"vtmp{par}",), writes=("kvb",))
            ctx.free_barrier([f"sw_w_vu{c}" for c in range(4)] + ["vtmp0", "vtmp1"])

            with ExitStack() as ws:
                wqu = load_w("w_qu", 512, 1024, ws)
                wqr = load_w("w_qr", 512, 1024, ws)
                qtmp = [sb(ws, nc, f"qtmp{i}", [128, NT], BF16) for i in range(2)]
                for qi in range(16):
                    par = qi % 2
                    wt = wqu if qi < 8 else wqr
                    wn = "sw_w_qu" if qi < 8 else "sw_w_qr"
                    bn = "b_qu" if qi < 8 else "b_qr"
                    kc = qi % 8
                    for tch in range(2):
                        bank = tch % 2

                        def mmq(te, kc=kc, tch=tch, bank=bank, wt=wt):
                            ins = None
                            for ch in range(4):
                                ins = te.matmul(ps[bank][:, :],
                                                wt[ch][:, 128 * kc:128 * (kc + 1)],
                                                qlat[ch][:, 512 * tch:512 * (tch + 1)],
                                                start=(ch == 0), stop=(ch == 3))
                            return ins
                        ctx.op("tensor", mmq,
                               reads=tuple(f"{wn}{c}" for c in range(4)) +
                                     tuple(f"qlat{c}" for c in range(4)),
                               writes=(f"pjps{bank}",))
                        ctx.op("vector", lambda v, kc=kc, tch=tch, bank=bank, par=par, bn=bn:
                               v.tensor_scalar(out=qtmp[par][:, 512 * tch:512 * (tch + 1)],
                                               in0=ps[bank][:, :],
                                               scalar1=bias[bn][:, kc:kc + 1],
                                               scalar2=None, op0=ALU.add),
                               reads=(f"pjps{bank}", bn), writes=(f"qtmp{par}",))
                    ctx.dma("gpsimd", qscr[128 * qi:128 * (qi + 1), :], qtmp[par][:, :],
                            reads=(f"qtmp{par}",), writes=("qscr",))
            ctx.free_barrier([f"sw_w_qu{c}" for c in range(4)] +
                             [f"sw_w_qr{c}" for c in range(4)] + ["qtmp0", "qtmp1"])
        ctx.free_barrier([f"xlnTo{c}" for c in range(16)] + [f"Ko{c}" for c in range(16)] +
                         [f"ckv{c}" for c in range(4)] + [f"qlat{c}" for c in range(4)] +
                         ["pjps0", "pjps1"])

        ctx.collective("AllGather", ALU.bypass, GP, [kvb[:]], [kvg[:]],
                       reads=("kvb",), writes=("kvg",))

        with ExitStack() as ast:
            Qh = [sb(ast, nc, f"Qh{p}", [128, NT], BF16) for p in range(2)]
            Kh = [sb(ast, nc, f"Kh{p}", [128, SEQ], BF16) for p in range(2)]
            Vh = [[sb(ast, nc, f"Vh{p}_{tc}", [128, 128], BF16, is_transpose_dest=True)
                   for tc in range(16)] for p in range(2)]
            Pbs = [sb(ast, nc, f"Pb{i}", [128, SEQ], BF16) for i in range(2)]
            PTs = [[sb(ast, nc, f"PT{i}_{tc}", [128, 128], BF16) for tc in range(16)]
                   for i in range(2)]
            msk = [sb(ast, nc, f"msk{j}", [128, 512], F32) for j in range(8)]
            rss = [sb(ast, nc, f"rs{i}", [128, 8], F32) for i in range(2)]
            ath = [sb(ast, nc, f"ath{p}", [128, NT], BF16) for p in range(2)]
            idt = sb(ast, nc, "idt", [128, 128], BF16)
            ctx.dma("gpsimd", idt[:, :], ident[:, :], writes=("idt",))
            psS = [ast.enter_context(nc.psum_tensor(f"psS{i}", [128, 512], F32))
                   for i in range(2)]
            psAs = [ast.enter_context(nc.psum_tensor(f"psA{i}", [128, 512], F32))
                    for i in range(2)]
            psT = [ast.enter_context(nc.psum_tensor(f"psT{i}", [128, 1024], BF16))
                   for i in range(2)]

            for j in range(8):
                ctx.dma("gpsimd", msk[j][:, :], masks[j, :, :], writes=(f"msk{j}",))

            def load_head(h):
                p = h % 2
                ctx.dma("gpsimd", Qh[p][:, :], qscr[128 * h:128 * (h + 1), :],
                        reads=("qscr",), writes=(f"Qh{p}",))
                for g in range(16):
                    s_, lt = SET_OF[g], LT_OF[g]
                    ctx.dma("gpsimd", Kh[p][:, 128 * g:128 * (g + 1)],
                            kvg[s_, 0, 128 * h:128 * (h + 1), 128 * lt:128 * (lt + 1)],
                            reads=("kvg",), writes=(f"Kh{p}",))
                for g in range(16):
                    s_, lt = SET_OF[g], LT_OF[g]
                    ctx.dma("sync" if g % 2 == 0 else "scalar", Vh[p][g][:, :],
                            kvg[s_, 1, 128 * h:128 * (h + 1), 128 * lt:128 * (lt + 1)],
                            transpose=True, reads=("kvg",), writes=(f"Vh{p}_{g}",))

            load_head(0)
            load_head(1)
            for h in range(16):
                p = h % 2
                for j in range(8):
                    e = EXT[j]
                    nkc = e // 512
                    pj = j % 2
                    Pb = Pbs[pj]
                    rs = rss[pj]
                    PT = PTs[pj]
                    psA = psAs[pj]
                    Pr = f"Pb{pj}"
                    rr = f"rs{pj}"
                    for kc in range(nkc):
                        bank = kc % 2
                        ctx.op("tensor", lambda te, j=j, kc=kc, bank=bank, p=p:
                               te.matmul(psS[bank][:, :], Qh[p][:, 128 * j:128 * (j + 1)],
                                         Kh[p][:, 512 * kc:512 * (kc + 1)],
                                         start=True, stop=True),
                               reads=(f"Qh{p}", f"Kh{p}"), writes=(f"psS{bank}",))
                        if kc == nkc - 1:
                            ctx.op("vector", lambda v, j=j, bank=bank: v.tensor_tensor(
                                out=psS[bank][:, :], in0=psS[bank][:, :], in1=msk[j][:, :],
                                op=ALU.add),
                                reads=(f"msk{j}", f"psS{bank}"), writes=(f"psS{bank}",))
                        ctx.op("scalar", lambda s, kc=kc, bank=bank, Pb=Pb, rs=rs: s.activation(
                            out=Pb[:, 512 * kc:512 * (kc + 1)], in_=psS[bank][:, :],
                            func=AF.Exp, scale=SCL, accum_out=rs[:, kc:kc + 1]),
                            reads=(f"psS{bank}",), writes=(Pr, rr))
                    ctx.op("vector", lambda v, nkc=nkc, rs=rs: v.reduce_sum(
                        out=rs[:, 6:7], in_=rs[:, 0:nkc], axis=AX),
                        reads=(rr,), writes=(rr,))
                    ctx.op("vector", lambda v, rs=rs: v.reciprocal(out=rs[:, 7:8], in_=rs[:, 6:7]),
                           reads=(rr,), writes=(rr,))
                    ctx.op("vector", lambda v, e=e, Pb=Pb, rs=rs: v.tensor_scalar(
                        out=Pb[:, 0:e], in0=Pb[:, 0:e], scalar1=rs[:, 7:8], scalar2=None,
                        op0=ALU.mult),
                        reads=(rr, Pr), writes=(Pr,))
                    for tc in range(e // 128):
                        tb = tc % 2
                        ctx.op("tensor", lambda te, tc=tc, tb=tb, Pb=Pb: te.transpose(
                            psT[tb][:, 0:128], Pb[:, 128 * tc:128 * (tc + 1)], idt[:, :]),
                            reads=(Pr, "idt"), writes=(f"psT{tb}",))
                        ctx.op("vector", lambda v, tc=tc, tb=tb, PT=PT: v.tensor_copy(
                            out=PT[tc][:, :], in_=psT[tb][:, 0:128]),
                            reads=(f"psT{tb}",), writes=(f"PT{pj}_{tc}",))

                    def mav(te, e=e, p=p, PT=PT, psA=psA):
                        ins = None
                        for tc in range(e // 128):
                            ins = te.matmul(psA[:, 0:128], Vh[p][tc][:, :], PT[tc][:, :],
                                            start=(tc == 0), stop=(tc == e // 128 - 1))
                        return ins
                    ctx.op("tensor", mav,
                           reads=tuple(f"Vh{p}_{tc}" for tc in range(e // 128)) +
                                 tuple(f"PT{pj}_{tc}" for tc in range(e // 128)),
                           writes=(f"psA{pj}",))
                    ctx.op("vector", lambda v, j=j, p=p, psA=psA: v.tensor_copy(
                        out=ath[p][:, 128 * j:128 * (j + 1)], in_=psA[:, 0:128]),
                        reads=(f"psA{pj}",), writes=(f"ath{p}",))
                ctx.dma("gpsimd", ascr[128 * h:128 * (h + 1), :], ath[p][:, :],
                        reads=(f"ath{p}",), writes=("ascr",))
                if h + 2 < 16:
                    load_head(h + 2)

        ctx.free_barrier(["psS0", "psS1", "psA0", "psA1", "psT0", "psT1",
                          "Pb0", "Pb1", "rs0", "rs1"])

        with ExitStack() as ost:
            ps = [ost.enter_context(nc.psum_tensor(f"wops{i}", [128, 512], F32))
                  for i in range(8)]
            wo = [sb(ost, nc, f"wo{a}", [128, D], BF16) for a in range(16)]
            atl = [sb(ost, nc, f"atl{a}", [128, 256], BF16) for a in range(16)]
            b_o_bc = sb(ost, nc, "b_o_bc", [128, D], F32)
            ctx.dma("gpsimd", b_o_bc[:, :],
                    dr["b_o"].rearrange("(a d) -> a d", a=1).broadcast_to((128, D)),
                    writes=("b_o_bc",))
            for a in range(16):
                ctx.dma("gpsimd", wo[a][:, :], dr["w_o"][128 * a:128 * (a + 1), :],
                        writes=(f"wo{a}",))
            for jp in range(4):
                for jl in range(2):
                    j = 2 * jp + jl
                    ctx.op("vector", lambda v, j=j: v.tensor_tensor(
                        out=xt[j][:], in0=xt[j][:], in1=b_o_bc[:], op=ALU.add),
                        reads=("b_o_bc",), writes=(f"xt{j}",))
                for a in range(16):
                    ctx.dma("sync", atl[a][:, :],
                            ascr[128 * a:128 * (a + 1), 256 * jp:256 * (jp + 1)],
                            reads=("ascr",), writes=(f"atl{a}",))

                def mmo(te, jp=jp):
                    ins = None
                    for a in range(16):
                        for jl in range(2):
                            for dc in range(4):
                                ins = te.matmul(ps[jl * 4 + dc][:, :],
                                                atl[a][:, 128 * jl:128 * (jl + 1)],
                                                wo[a][:, 512 * dc:512 * (dc + 1)],
                                                start=(a == 0), stop=(a == 15))
                    return ins
                ctx.op("tensor", mmo,
                       reads=tuple(f"atl{a}" for a in range(16)) +
                             tuple(f"wo{a}" for a in range(16)),
                       writes=tuple(f"wops{i}" for i in range(8)))
                for jl in range(2):
                    j = 2 * jp + jl
                    for dc in range(4):
                        ctx.op("vector", lambda v, j=j, jl=jl, dc=dc: v.scalar_tensor_tensor(
                            out=xt[j][:, 512 * dc:512 * (dc + 1)],
                            in0=ps[jl * 4 + dc][:, :], scalar=1.0,
                            in1=xt[j][:, 512 * dc:512 * (dc + 1)],
                            op0=ALU.mult, op1=ALU.add),
                            reads=(f"wops{jl * 4 + dc}",), writes=(f"xt{j}",))
        ctx.free_barrier(["b_o_bc"] + [f"wo{a}" for a in range(16)] +
                         [f"atl{a}" for a in range(16)] + [f"wops{i}" for i in range(8)])

        emit_ffn(ctx, nc, "f2", xt, st, sc_bc, xq_tmp, w1qd, w3qd, w2qd,
                 ("w1qd", "w3qd", "w2qd"))

        for t in range(8):
            ctx.dma("gpsimd", xo[128 * t:128 * (t + 1), :], xt[t][:], reads=(f"xt{t}",))
        ctx.finish("gpsimd")
        ctx.flush()
    return nc


def _mk_masks(tiles):
    m = np.zeros((8, 128, 512), dtype=np.float32)
    for j, g in enumerate(tiles):
        e = EXT[j]
        base = e - 512
        q = 128 * g + np.arange(128)[:, None]
        kk = base + np.arange(512)[None, :]
        m[j] = np.where(kk <= q, 0.0, NEGBIG)
    return m


_CACHE = {}


def _get(name, builder, *args):
    if name not in _CACHE:
        _CACHE[name] = builder(*args)
    return _CACHE[name]


def kernel(**inputs):
    x = np.ascontiguousarray(np.asarray(inputs["L0_attn_x"], dtype=np.float32))
    B = x.shape[0]
    cores = list(range(8))
    sets = [ASET, BSET]

    def own_rows(c):
        p, h = c // 2, c % 2
        return p, sets[h]

    w1 = np.ascontiguousarray(np.asarray(inputs["L0_ffn_w1"], dtype=np.float32))
    w3 = np.ascontiguousarray(np.asarray(inputs["L0_ffn_w3"], dtype=np.float32))
    w2 = np.ascontiguousarray(np.asarray(inputs["L0_ffn_w2"], dtype=np.float32))
    wnames = ["w_kv", "b_kv", "w_ku", "b_ku", "w_kr", "b_kr", "w_vu", "b_vu",
              "w_q", "b_q", "w_qu", "b_qu", "w_qr", "b_qr", "w_o", "b_o"]
    wmap = {n: np.ascontiguousarray(np.asarray(inputs["L0_attn_" + n], dtype=np.float32))
            for n in wnames}
    maskset = [_mk_masks(ASET), _mk_masks(BSET)]
    W1R = HID // 8
    ident = np.eye(128, dtype=ml_dtypes.bfloat16)

    ncF = _get("fused", build_fused)
    ins = []
    for c in cores:
        p, tl = own_rows(c)
        xs = np.ascontiguousarray(np.concatenate([x[p, 128 * t:128 * (t + 1), :] for t in tl], 0))
        d = {"x": xs,
             "w1s": np.ascontiguousarray(w1[W1R * c:W1R * (c + 1), :]),
             "w3s": np.ascontiguousarray(w3[W1R * c:W1R * (c + 1), :]),
             "w2s": np.ascontiguousarray(w2[:, W1R * c:W1R * (c + 1)]),
             "masks": maskset[c % 2], "ident": ident}
        d.update(wmap)
        ins.append(d)
    res = run_bass_kernel_spmd(ncF, ins, core_ids=cores).results

    out = np.empty((B, SEQ, D), dtype=np.float32)
    for c in cores:
        p, tl = own_rows(c)
        xoc = np.asarray(res[c]["xo"])
        for j, t in enumerate(tl):
            out[p, 128 * t:128 * (t + 1), :] = xoc[128 * j:128 * (j + 1), :]
    return out
